# version 01
# baseline (speedup 1.0000x reference)
import numpy as np
import ml_dtypes
from contextlib import ExitStack

import concourse.bass as bass
import concourse.mybir as mybir
from concourse.bass_utils import run_bass_kernel_spmd
from concourse.bass_core_rust import IndirectOffsetOnAxis

F32 = mybir.dt.float32
BF16 = mybir.dt.bfloat16
I32 = mybir.dt.int32
AF = mybir.ActivationFunctionType
ALU = mybir.AluOpType
AX = mybir.AxisListType.X
BF = ml_dtypes.bfloat16

NCORES = 8
BPC = 4
L = 2048
D = 1024
TPC = BPC * L
NH = 16
DH = 64
SK = 45
NE = 8
FF = 1408
FF2 = 2816
NKT = 8
NFT = 11
CT = 2048
NCHUNK = TPC // CT
BIGM = 3.0e4
NITER = 36
EPS = 1e-5
GBIG = 1.0e9

_SUBLAYERS = 3
_DBG = False
_DBGHW = False
_DBGB = 0


class _DryIns:
    def then_inc(self, *a, **k):
        return self


_DRY_INS = _DryIns()


class _DryEng:
    def __getattr__(self, name):
        def f(*a, **k):
            return _DRY_INS
        return f


_DRY = _DryEng()


class Sem:
    def __init__(self, ctx, nc, name):
        self.h = ctx.enter_context(nc.semaphore(name))
        self.v = 0
        self.name = name


class Plan:
    def __init__(self, block, sems):
        self.block = block
        self.sems = sems
        self.pass1 = True
        self.marks = {}

    def snapshot(self):
        return {s.name: s.v for s in self.sems}

    def restore(self, snap):
        for s in self.sems:
            s.v = snap[s.name]

    def M(self, tag, sem):
        if self.pass1:
            assert tag not in self.marks, f"dup mark {tag}"
            self.marks[tag] = (sem, sem.v)
        else:
            s, v = self.marks[tag]
            assert s is sem and v == sem.v, \
                f"mark drift {tag}: pass1={v} pass2={sem.v}"

    def W(self, eng, tag):
        if self.pass1:
            return
        sem, v = self.marks[tag]
        if v > 0:
            eng.wait_ge(sem.h, v)

    def inc(self, ins, sem, k=1):
        ins.then_inc(sem.h, k)
        sem.v += k

    def sec(self, name, f):
        if self.pass1:
            f(_DRY)
        else:
            getattr(self.block, name)(f)

    def unit(self, emit):
        snap = self.snapshot()
        self.pass1 = True
        emit()
        self.restore(snap)
        self.pass1 = False
        emit()


def _guard(p, eng, g):
    if p.pass1 or g is None:
        return
    for (sem, val) in g:
        if val > 0:
            eng.wait_ge(sem.h, val)


def build_program(sublayers=3):
    nc = bass.Bass("TRN2", target_bir_lowering=False, debug=False)

    h_x = nc.dram_tensor("x", [TPC, D], F32, kind="ExternalInput")
    h_y = nc.dram_tensor("y", [TPC, D], F32, kind="ExternalOutput")
    h_w1 = nc.dram_tensor("w1", [NE, D, FF2], F32, kind="ExternalInput")
    h_w2 = nc.dram_tensor("w2", [NE, FF, D], F32, kind="ExternalInput")
    h_gwT = nc.dram_tensor("gwT", [D, NE], F32, kind="ExternalInput")
    h_wqT = nc.dram_tensor("wqT", [D, D], F32, kind="ExternalInput")
    h_wkT = nc.dram_tensor("wkT", [D, D], F32, kind="ExternalInput")
    h_wvT = nc.dram_tensor("wvT", [D, D], F32, kind="ExternalInput")
    h_woT = nc.dram_tensor("woT", [D, D], F32, kind="ExternalInput")
    h_bq = nc.dram_tensor("bq", [128, NKT], F32, kind="ExternalInput")
    h_bk = nc.dram_tensor("bk", [128, NKT], F32, kind="ExternalInput")
    h_bv = nc.dram_tensor("bv", [128, NKT], F32, kind="ExternalInput")
    h_bo = nc.dram_tensor("bo_b", [128, D], F32, kind="ExternalInput")
    h_sidx = nc.dram_tensor("sidx_b", [128, SK], I32, kind="ExternalInput")
    h_idbf = nc.dram_tensor("ident_bf", [128, 128], BF16, kind="ExternalInput")
    h_idf = nc.dram_tensor("ident_f32", [128, 128], F32, kind="ExternalInput")
    h_io45 = nc.dram_tensor("iota45", [128, SK], F32, kind="ExternalInput")
    h_io2 = nc.dram_tensor("iota2", [128, 32], BF16, kind="ExternalInput")
    h_ioq = nc.dram_tensor("iotaq16", [128, 16], F32, kind="ExternalInput")
    h_ut = nc.dram_tensor("ut128", [128, 128], BF16, kind="ExternalInput")
    h_on = nc.dram_tensor("ones128", [128, 128], BF16, kind="ExternalInput")
    h_onr = nc.dram_tensor("onesrow", [1, 128], BF16, kind="ExternalInput")
    h_inv = nc.dram_tensor("invcol", [128, 1], BF16, kind="ExternalInput")
    h_eps = nc.dram_tensor("epscol", [128, 1], F32, kind="ExternalInput")

    h_w1b = nc.dram_tensor("w1b", [NE, D, FF2], BF16, kind="Internal")
    h_w2b = nc.dram_tensor("w2b", [NE, FF, D], BF16, kind="Internal")
    h_gwb = nc.dram_tensor("gwb", [D, NE], BF16, kind="Internal")
    h_wqb = nc.dram_tensor("wqb", [D, D], BF16, kind="Internal")
    h_wkb = nc.dram_tensor("wkb", [D, D], BF16, kind="Internal")
    h_wvb = nc.dram_tensor("wvb", [D, D], BF16, kind="Internal")
    h_wob = nc.dram_tensor("wob", [D, D], BF16, kind="Internal")
    h_xc1 = nc.dram_tensor("xc1", [TPC, D], F32, kind="Internal")
    h_xc2 = nc.dram_tensor("xc2", [TPC, D], F32, kind="Internal")
    h_kdd = nc.dram_tensor("kdd", [D, L], BF16, kind="Internal")
    h_dbgx = nc.dram_tensor("dbgx", [NKT, 128, CT], BF16, kind="Internal")
    if _DBGHW:
        h_dm = nc.dram_tensor("dbh_m", [16, L], F32, kind="ExternalOutput")
        h_ds = nc.dram_tensor("dbh_srch", [16, 8], F32, kind="ExternalOutput")
        h_di = nc.dram_tensor("dbh_idxi", [SK, 16], I32, kind="ExternalOutput")
        h_dc = nc.dram_tensor("dbh_ctxd", [L, D], F32, kind="ExternalOutput")
        h_da = nc.dram_tensor("dbh_attns", [SK, L], BF16, kind="ExternalOutput")
        h_dq = nc.dram_tensor("dbh_qt", [128, SK], BF16, kind="ExternalOutput")
        h_dn = nc.dram_tensor("dbh_negvm", [1, D], BF16, kind="ExternalOutput")
        h_dini = nc.dram_tensor("dbh_init", [128, D], F32, kind="ExternalOutput")
        h_dpt = nc.dram_tensor("dbh_pt", [SK, D], F32, kind="ExternalOutput")
        h_dkd = nc.dram_tensor("dbh_kd", [128, L], BF16, kind="ExternalOutput")
        h_dse = nc.dram_tensor("dbh_se", [SK, 4], F32, kind="ExternalOutput")
        h_dob = nc.dram_tensor("dbh_ob", [SK, 128], BF16, kind="ExternalOutput")
        h_dot = nc.dram_tensor("dbh_ot", [128, SK], BF16, kind="ExternalOutput")
        h_dwo = nc.dram_tensor("dbh_wo", [128, D], BF16, kind="ExternalOutput")
    h_dbgc = nc.dram_tensor("dbgc", [128, 128], F32, kind="Internal")
    h_dbga = nc.dram_tensor("dbga", [NFT, 128, CT], BF16, kind="Internal")
    h_ctxd = nc.dram_tensor("ctxd", [L, D], F32, kind="Internal")

    with ExitStack() as top:
        En = top.enter_context
        block = En(nc.Block())

        sems = []

        def mksem(name):
            s = Sem(top, nc, name)
            sems.append(s)
            return s

        sSX = mksem("sx")
        sSW = mksem("sw")
        sSO = mksem("so")
        sTE = mksem("ste")
        sT2 = mksem("st2")
        sVE = mksem("sve")
        sV2 = mksem("sv2")
        sSE = mksem("sse")
        sAG = mksem("sag")
        sVX = mksem("svx")
        sXA = [mksem(f"sxa{i}") for i in range(5)]
        sW1 = [mksem(f"sw1{i}") for i in range(2)]
        sW2D = mksem("sw2d")
        sKDW = [mksem(f"skdw{i}") for i in range(2)]
        sKDL = [mksem(f"skdl{i}") for i in range(2)]
        sWOL = [mksem(f"swol{i}") for i in range(2)]
        sWV = [mksem(f"swv{i}") for i in range(2)]
        sSC = mksem("ssc")
        sRB = [mksem(f"srb{i}") for i in range(2)]
        sRS = [mksem(f"srs{i}") for i in range(2)]
        sCI = [mksem(f"sci{i}") for i in range(2)]
        sCO = [mksem(f"sco{i}") for i in range(2)]

        p = Plan(block, sems)

        def vedge(ins, v):
            p.inc(ins, sVX)
            if not p.pass1:
                v.wait_ge(sVX.h, sVX.v)

        _uid = [0]

        def uname(n):
            _uid[0] += 1
            return f"{n}_u{_uid[0]}"

        def sbuf_list(E_, name, n, shape, dtype):
            return [E_(nc.sbuf_tensor(uname(f"{name}{i}"), shape, dtype))
                    for i in range(n)]

        def psum_list(E_, name, n, shape, dtype=F32):
            return [E_(nc.psum_tensor(uname(f"{name}{i}"), shape, dtype))
                    for i in range(n)]

        c_idbf = En(nc.sbuf_tensor("c_idbf", [128, 128], BF16))
        c_idf = En(nc.sbuf_tensor("c_idf", [128, 128], F32))
        c_io45 = En(nc.sbuf_tensor("c_io45", [128, SK], F32))
        c_io2 = En(nc.sbuf_tensor("c_io2", [128, 32], BF16))
        c_ioq = En(nc.sbuf_tensor("c_ioq", [128, 16], F32))
        c_ut = En(nc.sbuf_tensor("c_ut", [128, 128], BF16))
        c_on = En(nc.sbuf_tensor("c_on", [128, 128], BF16))
        c_onr = En(nc.sbuf_tensor("c_onr", [1, 128], BF16))
        c_inv = En(nc.sbuf_tensor("c_inv", [128, 1], BF16))
        c_eps = En(nc.sbuf_tensor("c_eps", [128, 1], F32))
        c_bq = En(nc.sbuf_tensor("c_bq", [128, NKT], F32))
        c_bk = En(nc.sbuf_tensor("c_bk", [128, NKT], F32))
        c_bv = En(nc.sbuf_tensor("c_bv", [128, NKT], F32))
        c_bo = En(nc.sbuf_tensor("c_bo", [128, D], F32))
        c_G = sbuf_list(En, "c_G", 16, [128, SK + 1], BF16)
        c_gw = sbuf_list(En, "c_gw", NKT, [128, NE], BF16)

        def _(sy):
            for dst, src in ((c_idbf, h_idbf), (c_idf, h_idf), (c_io45, h_io45),
                             (c_io2, h_io2), (c_ioq, h_ioq), (c_ut, h_ut),
                             (c_on, h_on), (c_onr, h_onr), (c_inv, h_inv),
                             (c_eps, h_eps), (c_bo, h_bo)):
                sy.dma_start(out=dst[:], in_=src.ap()).then_inc(sSW.h, 16)
            for dst, src in ((c_bq, h_bq), (c_bk, h_bk), (c_bv, h_bv)):
                sy.dma_start(out=dst[:], in_=src.ap()).then_inc(sSW.h, 16)
        block.sync(_)
        sSW.v += 16 * 14

        conv = []
        for e in range(NE):
            for k in range(NKT):
                conv.append((h_w1.ap()[e, k * 128:(k + 1) * 128, :],
                             h_w1b.ap()[e, k * 128:(k + 1) * 128, :], FF2))
        for e in range(NE):
            for k in range(NFT):
                conv.append((h_w2.ap()[e, k * 128:(k + 1) * 128, :],
                             h_w2b.ap()[e, k * 128:(k + 1) * 128, :], D))
        for hsrc, hdst in ((h_wqT, h_wqb), (h_wkT, h_wkb), (h_wvT, h_wvb),
                           (h_woT, h_wob)):
            for k in range(NKT):
                conv.append((hsrc.ap()[k * 128:(k + 1) * 128, :],
                             hdst.ap()[k * 128:(k + 1) * 128, :], D))
        for k in range(NKT):
            conv.append((h_gwT.ap()[k * 128:(k + 1) * 128, :],
                         h_gwb.ap()[k * 128:(k + 1) * 128, :], NE))
        ncv = len(conv)

        with nc.sbuf_tensor("cvin0", [128, FF2], F32) as cvin0, \
             nc.sbuf_tensor("cvin1", [128, FF2], F32) as cvin1, \
             nc.sbuf_tensor("cvout0", [128, FF2], BF16) as cvout0, \
             nc.sbuf_tensor("cvout1", [128, FF2], BF16) as cvout1:
            cvin = [cvin0, cvin1]
            cvout = [cvout0, cvout1]
            bSW = sSW.v
            bVE = sVE.v
            bAG = sAG.v

            def _(sy):
                for i in range(ncv):
                    cols = conv[i][2]
                    if i >= 2:
                        sy.wait_ge(sVE.h, bVE + i - 1)
                    sy.dma_start(out=cvin[i % 2][:, 0:cols],
                                 in_=conv[i][0]).then_inc(sCI[i % 2].h, 16)
                    sCI[i % 2].v += 16
            block.sync(_)

            def _(v):
                for i in range(ncv):
                    cols = conv[i][2]
                    v.wait_ge(sCI[i % 2].h, 16 * (i // 2 + 1))
                    if i >= 2:
                        v.wait_ge(sCO[i % 2].h, 16 * ((i - 2) // 2 + 1))
                    v.tensor_copy(cvout[i % 2][:, 0:cols],
                                  cvin[i % 2][:, 0:cols]).then_inc(sVE.h, 1)
            block.vector(_)
            sVE.v += ncv

            def _(g):
                for i in range(ncv):
                    cols = conv[i][2]
                    g.wait_ge(sVE.h, bVE + i + 1)
                    g.dma_start(out=conv[i][1],
                                in_=cvout[i % 2][:, 0:cols]).then_inc(
                                    sCO[i % 2].h, 16)
                    sCO[i % 2].v += 16
            block.gpsimd(_)

        def _(sy):
            sy.wait_ge(sCO[0].h, sCO[0].v)
            sy.wait_ge(sCO[1].h, sCO[1].v)
            for k in range(NKT):
                sy.dma_start(out=c_gw[k][:],
                             in_=h_gwb.ap()[k * 128:(k + 1) * 128, :]
                             ).then_inc(sSW.h, 16)
        block.sync(_)
        sSW.v += 16 * NKT

        with nc.sbuf_tensor("sidxi", [128, SK], I32) as sidxi, \
             nc.sbuf_tensor("sidxf", [128, SK], F32) as sidxf, \
             nc.sbuf_tensor("gred", [128, 2], F32) as gred:
            def _(sy):
                sy.dma_start(out=sidxi[:], in_=h_sidx.ap()).then_inc(sSW.h, 16)
            block.sync(_)
            sSW.v += 16
            bSW2 = sSW.v

            def _(v):
                v.wait_ge(sSW.h, bSW2)
                ins = v.tensor_copy(sidxf[:], sidxi[:])
                ins.then_inc(sVX.h, 1)
                v.wait_ge(sVX.h, sVX.v + 1)
                ins = v.tensor_single_scalar(sidxf[:], sidxf[:], float(L - 1),
                                             ALU.min)
                ins.then_inc(sVX.h, 1)
                v.wait_ge(sVX.h, sVX.v + 2)
                for qt in range(18):
                    if qt < 16:
                        v.tensor_scalar(c_G[qt][:, 0:SK], sidxf[:],
                                        c_ioq[:, qt:qt + 1], 0.125,
                                        ALU.is_equal, ALU.mult)
                    if 1 <= qt <= 16:
                        v.reduce_sum(gred[:, qt % 2:qt % 2 + 1],
                                     c_G[qt - 1][:, 0:SK], axis=AX)
                    if qt >= 2:
                        v.tensor_single_scalar(
                            c_G[qt - 2][:, SK:SK + 1],
                            gred[:, (qt - 1) % 2:(qt - 1) % 2 + 1],
                            1.0 / SK, ALU.mult).then_inc(sVE.h, 1)
            block.vector(_)
            sVX.v += 2
            sVE.v += 16

        prologue_guard = [(s, s.v) for s in sems]

        def ln_region(ph, r, src_ap, row0, xtile, keepx, xnbf, xnT, stats, sqj,
                      p_tr, entry, rprev=None):
            def _(g):
                _guard(p, g, entry)
                if keepx:
                    p.W(g, (ph, r, "xfree"))
                for t in range(16):
                    if not keepx and t >= 5:
                        p.W(g, (ph, r, "xn", t - 5))
                    ins = g.dma_start(
                        out=xtile(t),
                        in_=src_ap[row0 + t * 128:row0 + (t + 1) * 128, :])
                    if keepx:
                        p.inc(ins, sSX, 16)
                        if t == 15:
                            p.M((ph, r, "ldxall"), sSX)
                    else:
                        p.inc(ins, sXA[t % 5], 16)
                        p.M((ph, r, "ldx", t), sXA[t % 5])
                if keepx:
                    for t in range(16):
                        p.marks[(ph, r, "ldx", t)] = p.marks[(ph, r, "ldxall")]
            p.sec("gpsimd", _)

            def _(s):
                _guard(p, s, entry)
                for t in range(17):
                    if t < 16:
                        p.W(s, (ph, r, "ldx", t))
                        ins = s.activation(
                            sqj[:], xtile(t), AF.Square,
                            accum_out=stats[:, 12 * t + 1:12 * t + 2])
                        p.inc(ins, sSE)
                        p.M((ph, r, "sq", t), sSE)
                    if t >= 1:
                        u = t - 1
                        p.W(s, (ph, r, "var", u))
                        s.activation(stats[:, 12 * u + 5:12 * u + 6],
                                     stats[:, 12 * u + 4:12 * u + 5],
                                     AF.Ln, bias=c_eps[:, 0:1])
                        ins = s.activation(stats[:, 12 * u + 6:12 * u + 7],
                                           stats[:, 12 * u + 5:12 * u + 6],
                                           AF.Exp, scale=-0.5)
                        p.inc(ins, sSE)
                        p.M((ph, r, "rstd", u), sSE)
            p.sec("scalar", _)

            def _(v):
                _guard(p, v, entry)
                for t in range(21):
                    if t < 16:
                        p.W(v, (ph, r, "ldx", t))
                        v.reduce_sum(stats[:, 12 * t:12 * t + 1], xtile(t),
                                     axis=AX)
                    if 1 <= t <= 16:
                        j = t - 1
                        v.tensor_single_scalar(stats[:, 12 * j + 2:12 * j + 3],
                                               stats[:, 12 * j:12 * j + 1],
                                               1.0 / D, ALU.mult)
                    if 2 <= t <= 17:
                        j = t - 2
                        v.tensor_mul(stats[:, 12 * j + 3:12 * j + 4],
                                     stats[:, 12 * j + 2:12 * j + 3],
                                     stats[:, 12 * j + 2:12 * j + 3])
                    if 3 <= t <= 18:
                        j = t - 3
                        p.W(v, (ph, r, "sq", j))
                        ins = v.tensor_scalar(stats[:, 12 * j + 4:12 * j + 5],
                                              stats[:, 12 * j + 1:12 * j + 2],
                                              1.0 / D,
                                              stats[:, 12 * j + 3:12 * j + 4],
                                              ALU.mult, ALU.subtract)
                        p.inc(ins, sV2)
                        p.M((ph, r, "var", j), sV2)
                    if 4 <= t <= 19:
                        u = t - 4
                        p.W(v, (ph, r, "rstd", u))
                        if u >= 2:
                            p.W(v, (ph, r, "tr", u - 2))
                        elif rprev is not None:
                            p.W(v, (ph, rprev, "tr", 14 + u))
                        ins = v.tensor_scalar(xnbf[u % 2][:], xtile(u),
                                              stats[:, 12 * u + 2:12 * u + 3],
                                              stats[:, 12 * u + 6:12 * u + 7],
                                              ALU.subtract, ALU.mult)
                        p.inc(ins, sVE)
                        p.M((ph, r, "xn", u), sVE)
                    if t >= 5:
                        e_ = t - 5
                        p.W(v, (ph, r, "tr", e_))
                        for i in range(NKT):
                            ins = v.tensor_copy(
                                xnT[i][:, e_ * 128:(e_ + 1) * 128],
                                p_tr[e_ % 2][:, i * 128:(i + 1) * 128])
                        p.inc(ins, sV2)
                        p.M((ph, r, "evac", e_), sV2)
                p.M((ph, r, "evacall"), sV2)
            p.sec("vector", _)

            def _(t_):
                _guard(p, t_, entry)
                for t in range(16):
                    p.W(t_, (ph, r, "xn", t))
                    if t >= 2:
                        p.W(t_, (ph, r, "evac", t - 2))
                    elif rprev is not None:
                        p.W(t_, (ph, rprev, "evac", 14 + t))
                    for i in range(NKT):
                        ins = t_.transpose(p_tr[t % 2][:, i * 128:(i + 1) * 128],
                                           xnbf[t % 2][:, i * 128:(i + 1) * 128],
                                           c_idbf[:])
                    p.inc(ins, sTE)
                    p.M((ph, r, "tr", t), sTE)
            p.sec("tensor", _)



        def run_ffn(ph, src_ap, dst_ap, entry):
            with ExitStack() as fs:
                Ef = fs.enter_context
                xt = sbuf_list(Ef, "f_xt", 16, [128, D], F32)
                xnbf = sbuf_list(Ef, "f_xnbf", 2, [128, D], BF16)
                xnT = sbuf_list(Ef, "f_xnT", NKT, [128, CT], BF16)
                stats = Ef(nc.sbuf_tensor(uname("f_stats"), [128, 200], F32))
                sqj = Ef(nc.sbuf_tensor(uname("f_sqj"), [128, D], BF16))
                ab = sbuf_list(Ef, "f_ab", NFT, [128, CT], BF16)
                w1sf = sbuf_list(Ef, "f_w1s", 4, [128, 1024], BF16)
                w1s = {(0, 0): w1sf[0], (0, 1): w1sf[1],
                       (1, 0): w1sf[2], (1, 1): w1sf[3]}
                w2s = sbuf_list(Ef, "f_w2s", NFT, [128, D], BF16)
                sgb = sbuf_list(Ef, "f_sgb", 2, [128, 512], BF16)
                cbuf = Ef(nc.sbuf_tensor(uname("f_cbuf"), [128, 128], F32))
                gsb = sbuf_list(Ef, "f_gsb", 2, [128, NE], F32)
                g8a = Ef(nc.sbuf_tensor(uname("f_g8a"), [128, NE], F32))
                g8b = Ef(nc.sbuf_tensor(uname("f_g8b"), [128, NE], F32))
                g8c = Ef(nc.sbuf_tensor(uname("f_g8c"), [128, NE], F32))
                gi1 = sbuf_list(Ef, "f_gi1", 2, [128, NE], F32)
                gi2 = sbuf_list(Ef, "f_gi2", 2, [128, NE], F32)
                gw1p = Ef(nc.sbuf_tensor(uname("f_gw1p"), [128, 1], F32))
                gw2p = Ef(nc.sbuf_tensor(uname("f_gw2p"), [128, 1], F32))
                p_tr = psum_list(Ef, "f_ptr", 2, [128, 1024], BF16)
                p_g = Ef(nc.psum_tensor(uname("f_pg"), [128, 512], F32))
                p_u = Ef(nc.psum_tensor(uname("f_pu"), [128, 512], F32))
                p_y = psum_list(Ef, "f_py", 2, [128, 512], F32)
                p_ga = Ef(nc.psum_tensor(uname("f_pga"), [128, NE], F32))

                def m_glob(c, e, fp, ncc):
                    return ((c * NE + e) * NFT + fp) * 4 + ncc

                def y_glob(c, e, tg, dc):
                    return ((c * NE + e) * 16 + tg) * 2 + dc

                def emit():
                    def _(sy):
                        _guard(p, sy, entry)
                        for c in range(NCHUNK):
                            for e in range(NE):
                                for fp in range(NFT):
                                    si = (c * NE + e) * NFT + fp
                                    if si >= 2:
                                        p.W(sy, (ph, "mm1u", si - 2))
                                    for gi, f in enumerate((fp, NFT + fp)):
                                        src = bass.AP(
                                            h_w1b, e * D * FF2 + f * 128,
                                            [[FF2, 128], [128 * FF2, NKT],
                                             [1, 128]])
                                        dst = w1s[(si % 2, gi)][:].rearrange(
                                            "p (k c) -> p k c", c=128)
                                        ins = sy.dma_start(out=dst, in_=src)
                                        p.inc(ins, sW1[si % 2], 16)
                                    p.M((ph, "w1", si), sW1[si % 2])
                                ei = c * NE + e
                                if ei >= 1:
                                    p.W(sy, (ph, "mm2last", ei - 1))
                                for k in range(NFT):
                                    ins = sy.dma_start(
                                        out=w2s[k][:],
                                        in_=h_w2b.ap()[e, k * 128:(k + 1) * 128, :])
                                    p.inc(ins, sW2D, 16)
                                p.M((ph, "w2", ei), sW2D)
                    p.sec("sync", _)

                    for c in range(NCHUNK):
                        ln_region(ph, c, src_ap, c * CT, lambda t: xt[t][:], True,
                                  xnbf, xnT, stats, sqj, p_tr,
                                  entry if c == 0 else None,
                                  rprev=(c - 1) if c > 0 else None)
                        emit_chunk(c)

                        def _(g, c=c):
                            if _DBG and ph == "F1" and c == 0:
                                p.W(g, (ph, c, "evacall"))
                                p.W(g, (ph, c, "gc", 15))
                                for i in range(NKT):
                                    ins = g.dma_start(
                                        out=h_dbgx.ap()[i], in_=xnT[i][:])
                                    p.inc(ins, sAG, 16)
                                ins = g.dma_start(out=h_dbgc.ap(), in_=cbuf[:])
                                p.inc(ins, sAG, 16)
                                p.W(g, (ph, "actall", c * NE + 0))
                                for i in range(NFT):
                                    ins = g.dma_start(
                                        out=h_dbga.ap()[i], in_=ab[i][:])
                                    p.inc(ins, sAG, 16)
                            for t in range(16):
                                p.W(g, (ph, c, "sttlast", t))
                                ins = g.dma_start(
                                    out=dst_ap[c * CT + t * 128:
                                               c * CT + (t + 1) * 128, :],
                                    in_=xt[t][:])
                                p.inc(ins, sSO, 16)
                                p.M((ph, c, "sto", t), sSO)
                            p.M((ph, c, "stoall"), sSO)
                        p.sec("gpsimd", _)

                def emit_chunk(c):
                    def _(t_):
                        if c == 0:
                            _guard(p, t_, entry)
                        if True:
                            for t in range(16):
                                g = c * 16 + t
                                p.W(t_, (ph, c, "evac", t))
                                if g >= 1:
                                    p.W(t_, (ph, "gcp", g - 1))
                                for k in range(NKT):
                                    ins = t_.matmul(
                                        p_ga[:],
                                        xnT[k][:, t * 128:(t + 1) * 128],
                                        c_gw[k][:],
                                        start=(k == 0), stop=(k == NKT - 1))
                                p.inc(ins, sTE)
                                p.M((ph, "gmm", g), sTE)
                            for e in range(NE):
                                ei = c * NE + e
                                for fp in range(NFT):
                                    si = ei * NFT + fp
                                    for ncc in range(4):
                                        m = m_glob(c, e, fp, ncc)
                                        if m >= 1:
                                            p.W(t_, (ph, "sil", m - 1))
                                        p.W(t_, (ph, "w1", si))
                                        p.W(t_, (ph, c, "evacall"))
                                        for k in range(NKT):
                                            ins = t_.matmul(
                                                p_g[:],
                                                w1s[(si % 2, 0)][:, k * 128:(k + 1) * 128],
                                                xnT[k][:, ncc * 512:(ncc + 1) * 512],
                                                start=(k == 0),
                                                stop=(k == NKT - 1))
                                        p.inc(ins, sTE)
                                        p.M((ph, "mm1g", m), sTE)
                                        if m >= 1:
                                            p.W(t_, (ph, "act", m - 1))
                                        for k in range(NKT):
                                            ins = t_.matmul(
                                                p_u[:],
                                                w1s[(si % 2, 1)][:, k * 128:(k + 1) * 128],
                                                xnT[k][:, ncc * 512:(ncc + 1) * 512],
                                                start=(k == 0),
                                                stop=(k == NKT - 1))
                                        p.inc(ins, sTE)
                                        p.M((ph, "mm1u_m", m), sTE)
                                        if ncc == 3:
                                            p.M((ph, "mm1u", si), sTE)
                                for tg in range(16):
                                    for dc in range(2):
                                        yi = y_glob(c, e, tg, dc)
                                        if yi >= 2:
                                            p.W(t_, (ph, "stt", yi - 2))
                                        p.W(t_, (ph, "actall", ei))
                                        p.W(t_, (ph, "w2", ei))
                                        for k in range(NFT):
                                            ins = t_.matmul(
                                                p_y[yi % 2][:],
                                                ab[k][:, tg * 128:(tg + 1) * 128],
                                                w2s[k][:, dc * 512:(dc + 1) * 512],
                                                start=(k == 0),
                                                stop=(k == NFT - 1))
                                        p.inc(ins, sT2)
                                        p.M((ph, "mm2", yi), sT2)
                                p.M((ph, "mm2last", ei), sT2)
                    p.sec("tensor", _)

                    def _(s):
                        if c == 0:
                            _guard(p, s, entry)
                        if True:
                            for t in range(16):
                                p.W(s, (ph, c, "gd", t))
                                ins = s.activation(stats[:, 12 * t + 8:12 * t + 9],
                                                   stats[:, 12 * t + 7:12 * t + 8],
                                                   AF.Sigmoid)
                                p.inc(ins, sSE)
                                p.M((ph, c, "sig", t), sSE)
                            for e in range(NE):
                                for fp in range(NFT):
                                    for ncc in range(4):
                                        m = m_glob(c, e, fp, ncc)
                                        p.W(s, (ph, "mm1g", m))
                                        if m >= 2:
                                            p.W(s, (ph, "act", m - 2))
                                        ins = s.activation(sgb[m % 2][:], p_g[:],
                                                           AF.Silu)
                                        p.inc(ins, sSE)
                                        p.M((ph, "sil", m), sSE)
                    p.sec("scalar", _)

                    def _(v):
                        if c == 0:
                            _guard(p, v, entry)
                        if True:
                            for t in range(17):
                                tv = t < 16
                                uv = t >= 1
                                u = t - 1
                                g = c * 16 + t
                                ca = 12 * t
                                cu = 12 * u
                                if tv:
                                    p.W(v, (ph, "gmm", g))
                                    ins = v.tensor_copy(gsb[t % 2][:], p_ga[:])
                                    p.inc(ins, sVE)
                                    p.M((ph, "gcp", g), sVE)
                                    if not uv and not p.pass1:
                                        v.wait_ge(sVE.h, sVE.v)
                                if uv:
                                    p.W(v, (ph, c, "sig", u))
                                    ins = v.tensor_single_scalar(
                                        gw1p[:], stats[:, cu + 8:cu + 9],
                                        0.5, ALU.mult)
                                    if not tv:
                                        vedge(ins, v)
                                if tv:
                                    ins = v.reduce_max(stats[:, ca + 9:ca + 10],
                                                       gsb[t % 2][:], axis=AX)
                                    if not uv:
                                        vedge(ins, v)
                                if uv:
                                    v.tensor_scalar(gw2p[:], gw1p[:], -1.0, 0.5,
                                                    ALU.mult, ALU.add)
                                if tv:
                                    ins = v.tensor_scalar(g8a[:], gsb[t % 2][:],
                                                          stats[:, ca + 9:ca + 10],
                                                          None, ALU.is_equal)
                                    if not uv:
                                        vedge(ins, v)
                                if uv:
                                    ins = v.tensor_scalar(g8c[:], gi1[u % 2][:],
                                                          gw1p[:, 0:1], None,
                                                          ALU.mult)
                                    if not tv:
                                        vedge(ins, v)
                                if tv:
                                    ins = v.scalar_tensor_tensor(g8b[:], g8a[:],
                                                                 -GBIG,
                                                                 gsb[t % 2][:],
                                                                 ALU.mult, ALU.add)
                                    if not uv:
                                        vedge(ins, v)
                                if uv:
                                    ins = v.scalar_tensor_tensor(
                                        cbuf[:, u * 8:(u + 1) * 8],
                                        gi2[u % 2][:], gw2p[:, 0:1], g8c[:],
                                        ALU.mult, ALU.add)
                                    p.inc(ins, sVE)
                                    p.M((ph, c, "gc", u), sVE)
                                if tv:
                                    v.reduce_max(stats[:, ca + 10:ca + 11],
                                                 g8b[:], axis=AX)
                                    v.tensor_copy(gi1[t % 2][:], g8a[:])
                                    ins = v.tensor_sub(
                                        stats[:, ca + 7:ca + 8],
                                        stats[:, ca + 9:ca + 10],
                                        stats[:, ca + 10:ca + 11])
                                    p.inc(ins, sVE)
                                    p.M((ph, c, "gd", t), sVE)
                                    v.tensor_scalar(gi2[t % 2][:], g8b[:],
                                                    stats[:, ca + 10:ca + 11],
                                                    None, ALU.is_equal)
                            for e in range(NE):
                                ei = c * NE + e
                                for fp in range(NFT):
                                    for ncc in range(4):
                                        m = m_glob(c, e, fp, ncc)
                                        p.W(v, (ph, "sil", m))
                                        p.W(v, (ph, "mm1u_m", m))
                                        if ei >= 1:
                                            p.W(v, (ph, "mm2last", ei - 1))
                                        ins = v.tensor_mul(
                                            ab[fp][:, ncc * 512:(ncc + 1) * 512],
                                            sgb[m % 2][:], p_u[:])
                                        p.inc(ins, sVE)
                                        p.M((ph, "act", m), sVE)
                                p.M((ph, "actall", ei), sVE)
                                for tg in range(16):
                                    for dc in range(2):
                                        yi = y_glob(c, e, tg, dc)
                                        p.W(v, (ph, "mm2", yi))
                                        p.W(v, (ph, c, "gc", tg))
                                        ins = v.scalar_tensor_tensor(
                                            xt[tg][:, dc * 512:(dc + 1) * 512],
                                            p_y[yi % 2][:],
                                            cbuf[:, tg * 8 + e:tg * 8 + e + 1],
                                            xt[tg][:, dc * 512:(dc + 1) * 512],
                                            ALU.mult, ALU.add)
                                        p.inc(ins, sVE)
                                        p.M((ph, "stt", yi), sVE)
                                        if e == NE - 1 and dc == 1:
                                            p.M((ph, c, "sttlast", tg), sVE)
                    p.sec("vector", _)

                def wire_xfree():
                    for c in range(NCHUNK):
                        tag = (ph, c, "xfree")
                        if c == 0:
                            p.marks[tag] = (sSO, 0)
                        else:
                            p.marks[tag] = p.marks[(ph, c - 1, "stoall")]

                snap = p.snapshot()
                p.pass1 = True
                emit()
                wire_xfree()
                p.restore(snap)
                p.pass1 = False
                emit()
            return [(s, s.v) for s in sems]

        def run_attn(ph, src_ap, dst_ap, entry):
            with ExitStack() as As:
                Ea = As.enter_context
                tks = sbuf_list(Ea, "a_tks", 16, [128, D], BF16)
                vtk = sbuf_list(Ea, "a_vtk", 16, [128, D], BF16)
                qd = sbuf_list(Ea, "a_qd", NKT, [128, L], BF16)
                ksT = sbuf_list(Ea, "a_ksT", 8, [128, 46], BF16)
                mt = sbuf_list(Ea, "a_mt", 16, [128, 16], F32)
                posP = sbuf_list(Ea, "a_posP", 16, [128, 16], F32)
                idxi = Ea(nc.sbuf_tensor("a_idxi", [SK, 16], I32))
                negvm = Ea(nc.sbuf_tensor("a_negvm", [1, D], BF16))
                otbuf = sbuf_list(Ea, "a_otbuf", 2, [SK, 128], BF16)
                srch = Ea(nc.sbuf_tensor("a_srch", [16, 8], F32))

                batch_guards = {}
                for b in range(BPC):
                    row0 = b * L
                    ent = entry if b == 0 else batch_guards[b - 1]

                    with ExitStack() as ps:
                        Ep = ps.enter_context
                        xt2 = sbuf_list(Ep, "ap_xt", 5, [128, D], F32)
                        xnbf = sbuf_list(Ep, "ap_xnbf", 2, [128, D], BF16)
                        xnT = sbuf_list(Ep, "ap_xnT", NKT, [128, L], BF16)
                        stats = Ep(nc.sbuf_tensor(uname("ap_stats"), [128, 200], F32))
                        sqj = Ep(nc.sbuf_tensor(uname("ap_sqj"), [128, D], BF16))
                        wmat = sbuf_list(Ep, "ap_wmat", NKT, [128, D], BF16)
                        kdtmp = sbuf_list(Ep, "ap_kdtmp", 2, [128, 512], BF16)
                        p_tr = psum_list(Ep, "ap_ptr", 2, [128, 1024], BF16)
                        p_po = psum_list(Ep, "ap_po", 2, [128, 512], F32)
                        p_ttr = psum_list(Ep, "ap_ttr", 2, [128, 512], BF16)

                        def emitP():
                            r = (b, "ln")
                            ln_region(ph, r, src_ap, row0,
                                      lambda t: xt2[t % 5][:], False,
                                      xnbf, xnT, stats, sqj, p_tr, ent)

                            mats = (("K", h_wkb, c_bk), ("V", h_wvb, c_bv),
                                    ("Q", h_wqb, c_bq))

                            def _(sy):
                                _guard(p, sy, ent)
                                for mi, (mn, hw, bias) in enumerate(mats):
                                    for k in range(NKT):
                                        if mi >= 1:
                                            p.W(sy, (ph, b, "pmmlast", mi - 1))
                                        ins = sy.dma_start(
                                            out=wmat[k][:],
                                            in_=hw.ap()[k * 128:(k + 1) * 128, :])
                                        p.inc(ins, sSW, 16)
                                    p.M((ph, b, "wmat", mi), sSW)
                            p.sec("sync", _)

                            def _(t_):
                                _guard(p, t_, ent)
                                for mi, (mn, hw, bias) in enumerate(mats):
                                    for ch in range(4):
                                        for m in range(NKT):
                                            gi = mi * 32 + ch * NKT + m
                                            pi = gi % 2
                                            p.W(t_, (ph, b, "wmat", mi))
                                            p.W(t_, (ph, (b, "ln"), "evacall"))
                                            if gi >= 2:
                                                p.W(t_, (ph, b, "pevd", gi - 2))
                                            for k in range(NKT):
                                                ins = t_.matmul(
                                                    p_po[pi][:],
                                                    wmat[k][:, m * 128:(m + 1) * 128],
                                                    xnT[k][:, ch * 512:(ch + 1) * 512],
                                                    start=(k == 0),
                                                    stop=(k == NKT - 1))
                                            p.inc(ins, sTE)
                                            p.M((ph, b, "pmm", gi), sTE)
                                            if mn != "Q":
                                                p.W(t_, (ph, b, "pevd", gi))
                                                if gi >= 2:
                                                    p.W(t_, (ph, b, "ttrev", gi - 2))
                                                for j in range(4):
                                                    ins = t_.transpose(
                                                        p_ttr[pi][:, j * 128:(j + 1) * 128],
                                                        kdtmp[pi][:, j * 128:(j + 1) * 128],
                                                        c_idbf[:])
                                                p.inc(ins, sTE)
                                                p.M((ph, b, "ttr", gi), sTE)
                                    p.M((ph, b, "pmmlast", mi), sTE)
                            p.sec("tensor", _)

                            def _(s):
                                _guard(p, s, ent)
                                for mi, (mn, hw, bias) in enumerate(mats):
                                    for ch in range(4):
                                        for m in range(NKT):
                                            gi = mi * 32 + ch * NKT + m
                                            pi = gi % 2
                                            p.W(s, (ph, b, "pmm", gi))
                                            if mn == "Q":
                                                dst = qd[m][:, ch * 512:(ch + 1) * 512]
                                            else:
                                                if gi >= 2:
                                                    p.W(s, (ph, b, "ttr", gi - 2))
                                                    if mn == "K":
                                                        p.W(s, (ph, b, "kdd", gi - 2))
                                                dst = kdtmp[pi][:]
                                            ins = s.activation(dst, p_po[pi][:],
                                                               AF.Identity,
                                                               bias=bias[:, m:m + 1])
                                            p.inc(ins, sSE)
                                            p.M((ph, b, "pevd", gi), sSE)
                            p.sec("scalar", _)

                            def _(g):
                                for ch in range(4):
                                    for m in range(NKT):
                                        gi = ch * NKT + m
                                        pi = gi % 2
                                        p.W(g, (ph, b, "pevd", gi))
                                        ins = g.dma_start(
                                            out=h_kdd.ap()[m * 128:(m + 1) * 128,
                                                           ch * 512:(ch + 1) * 512],
                                            in_=kdtmp[pi][:])
                                        p.inc(ins, sKDW[gi % 2], 16)
                                        p.M((ph, b, "kdd", gi), sKDW[gi % 2])
                                p.M((ph, b, "kddall0"), sKDW[0])
                                p.M((ph, b, "kddall1"), sKDW[1])
                            p.sec("gpsimd", _)

                            def _(v):
                                for mi, (mn, hw, bias) in enumerate(mats):
                                    if mn == "Q":
                                        continue
                                    tgt = tks if mn == "K" else vtk
                                    for ch in range(4):
                                        for m in range(NKT):
                                            gi = mi * 32 + ch * NKT + m
                                            pi = gi % 2
                                            p.W(v, (ph, b, "ttr", gi))
                                            for j in range(4):
                                                tok = ch * 4 + j
                                                ins = v.tensor_copy(
                                                    tgt[tok][:, m * 128:(m + 1) * 128],
                                                    p_ttr[pi][:, j * 128:(j + 1) * 128])
                                            p.inc(ins, sV2)
                                            p.M((ph, b, "ttrev", gi), sV2)
                                    p.M((ph, b, "tokdone", mn), sV2)
                            p.sec("vector", _)

                        p.unit(emitP)

                    with ExitStack() as ss:
                        Es = ss.enter_context
                        kssb = Es(nc.sbuf_tensor(uname("as_kssb"), [46, 128], BF16))
                        mrow = Es(nc.sbuf_tensor(uname("as_mrow"), [16, L], F32))
                        junkb = Es(nc.sbuf_tensor(uname("as_junkb"), [16, L], BF16))
                        selr = Es(nc.sbuf_tensor(uname("as_selr"), [16, L], BF16))
                        selT = sbuf_list(Es, "as_selT", 16, [128, 16], BF16)
                        mred = Es(nc.sbuf_tensor(uname("as_mred"), [128, 2], F32))
                        p_ks = Es(nc.psum_tensor(uname("as_pks"), [46, 128], F32))
                        p_kst = Es(nc.psum_tensor(uname("as_pkst"), [128, 46], BF16))
                        p_st = psum_list(Es, "as_pst", 2, [128, 46], F32)
                        p_mtr = psum_list(Es, "as_pmtr", 1, [16, 128], F32)
                        p_str = psum_list(Es, "as_pstr", 1, [128, 16], BF16)
                        p_pos = psum_list(Es, "as_ppos", 2, [128, 16], F32)

                        def emitS():
                            def _(t_):
                                for pr in range(9):
                                    if pr < 8:
                                        p.W(t_, (ph, b, "tokdone", "K"))
                                        if pr >= 1:
                                            p.W(t_, (ph, b, "ksv", pr - 1))
                                        for st in range(16):
                                            ins = t_.matmul(
                                                p_ks[0:46, :], c_G[st][:],
                                                tks[st][:, pr * 128:(pr + 1) * 128],
                                                start=(st == 0), stop=(st == 15))
                                        p.inc(ins, sTE)
                                        p.M((ph, b, "ksmm", pr), sTE)
                                    if pr >= 1:
                                        u_ = pr - 1
                                        p.W(t_, (ph, b, "ksv", u_))
                                        if u_ >= 1:
                                            p.W(t_, (ph, b, "ksev", u_ - 1))
                                        ins = t_.transpose(p_kst[:, 0:46],
                                                           kssb[:],
                                                           c_idbf[0:46, 0:46])
                                        p.inc(ins, sTE)
                                        p.M((ph, b, "kstr", u_), sTE)
                            p.sec("tensor", _)

                            def _(s):
                                for pr in range(9):
                                    if pr < 8:
                                        p.W(s, (ph, b, "ksmm", pr))
                                        if pr >= 1:
                                            p.W(s, (ph, b, "kstr", pr - 1))
                                        ins = s.activation(kssb[:],
                                                           p_ks[0:46, :],
                                                           AF.Copy)
                                        p.inc(ins, sSE)
                                        p.M((ph, b, "ksv", pr), sSE)
                                    if pr >= 1:
                                        u_ = pr - 1
                                        p.W(s, (ph, b, "kstr", u_))
                                        ins = s.activation(ksT[u_][:, 0:46],
                                                           p_kst[:, 0:46],
                                                           AF.Copy)
                                        p.inc(ins, sSE)
                                        p.M((ph, b, "ksev", u_), sSE)
                            p.sec("scalar", _)

                            def _(t_):
                                for bh in range(16):
                                    pr, hi = bh // 2, bh % 2
                                    base = 64 * hi
                                    for qt in range(16):
                                        i = bh * 16 + qt
                                        p.W(t_, (ph, b, "ksev", pr))
                                        p.W(t_, (ph, b, "pevd", 95))
                                        if i >= 2:
                                            p.W(t_, (ph, b, "mred", i - 2))
                                        ins = t_.matmul(
                                            p_st[i % 2][:],
                                            qd[pr][base:base + 64,
                                                   qt * 128:(qt + 1) * 128],
                                            ksT[pr][base:base + 64, 0:46],
                                            start=True, stop=True)
                                        p.inc(ins, sTE)
                                        p.M((ph, b, "stmm", i), sTE)
                            p.sec("tensor", _)

                            def _(v):
                                for i in range(257):
                                    if i < 256:
                                        p.W(v, (ph, b, "stmm", i))
                                        v.reduce_max(mred[:, i % 2:i % 2 + 1],
                                                     p_st[i % 2][:, 0:SK],
                                                     axis=AX)
                                    if i >= 1:
                                        j = i - 1
                                        ins = v.tensor_sub(
                                            mt[j % 16][:, j // 16:j // 16 + 1],
                                            mred[:, j % 2:j % 2 + 1],
                                            p_st[j % 2][:, SK:SK + 1])
                                        p.inc(ins, sVE)
                                        p.M((ph, b, "mred", j), sVE)
                                p.M((ph, b, "mall"), sVE)
                            p.sec("vector", _)

                            def _(t_):
                                p.W(t_, (ph, b, "mall"))
                                for qt in range(16):
                                    if qt >= 1:
                                        p.W(t_, (ph, b, "mtev", qt - 1))
                                    ins = t_.transpose(p_mtr[0][0:16, :],
                                                       mt[qt][:], c_idf[:])
                                    p.inc(ins, sTE)
                                    p.M((ph, b, "mtr", qt), sTE)
                            p.sec("tensor", _)

                            def _(s):
                                for qt in range(16):
                                    p.W(s, (ph, b, "mtr", qt))
                                    ins = s.activation(
                                        mrow[:, qt * 128:(qt + 1) * 128],
                                        p_mtr[0][0:16, :], AF.Copy)
                                    p.inc(ins, sSE)
                                    p.M((ph, b, "mtev", qt), sSE)
                                p.M((ph, b, "mrowdone"), sSE)
                            p.sec("scalar", _)

                            def _(v):
                                p.W(v, (ph, b, "mrowdone"))
                                v.memset(srch[:, 0:1], -BIGM)
                                ins = v.memset(srch[:, 1:2], BIGM)
                                vedge(ins, v)
                                for it in range(NITER):
                                    ins = v.tensor_scalar(srch[:, 2:3],
                                                          srch[:, 0:1],
                                                          srch[:, 1:2], 0.5,
                                                          ALU.add, ALU.mult)
                                    vedge(ins, v)
                                    ins = v.tensor_scalar(junkb[:], mrow[:],
                                                          srch[:, 2:3], 0.0,
                                                          ALU.is_gt, ALU.add,
                                                          accum_out=srch[:, 3:4])
                                    vedge(ins, v)
                                    ins = v.tensor_single_scalar(srch[:, 4:5],
                                                                 srch[:, 3:4],
                                                                 float(SK),
                                                                 ALU.is_ge)
                                    vedge(ins, v)
                                    v.tensor_sub(srch[:, 5:6], srch[:, 2:3],
                                                 srch[:, 0:1])
                                    ins = v.tensor_sub(srch[:, 6:7], srch[:, 1:2],
                                                       srch[:, 2:3])
                                    vedge(ins, v)
                                    v.tensor_mul(srch[:, 5:6], srch[:, 4:5],
                                                 srch[:, 5:6])
                                    ins = v.tensor_mul(srch[:, 6:7], srch[:, 4:5],
                                                       srch[:, 6:7])
                                    vedge(ins, v)
                                    v.tensor_add(srch[:, 0:1], srch[:, 0:1],
                                                 srch[:, 5:6])
                                    ins = v.tensor_add(srch[:, 1:2], srch[:, 2:3],
                                                       srch[:, 6:7])
                                    vedge(ins, v)
                                ins = v.tensor_scalar(selr[:], mrow[:],
                                                      srch[:, 0:1], None,
                                                      ALU.is_gt)
                                p.inc(ins, sVE)
                                p.M((ph, b, "sel"), sVE)
                            p.sec("vector", _)

                            def _(t_):
                                p.W(t_, (ph, b, "sel"))
                                for qt in range(16):
                                    if qt >= 1:
                                        p.W(t_, (ph, b, "selev", qt - 1))
                                    ins = t_.transpose(
                                        p_str[0][:, 0:16],
                                        selr[0:16, qt * 128:(qt + 1) * 128],
                                        c_idbf[0:16, 0:16])
                                    p.inc(ins, sTE)
                                    p.M((ph, b, "seltr", qt), sTE)
                            p.sec("tensor", _)

                            def _(s):
                                for qt in range(16):
                                    p.W(s, (ph, b, "seltr", qt))
                                    ins = s.activation(selT[qt][:],
                                                       p_str[0][:, 0:16],
                                                       AF.Copy)
                                    p.inc(ins, sSE)
                                    p.M((ph, b, "selev", qt), sSE)
                            p.sec("scalar", _)

                            def _(t_):
                                for qt in range(16):
                                    p.W(t_, (ph, b, "selev", qt))
                                    if qt >= 2:
                                        p.W(t_, (ph, b, "posev", qt - 2))
                                    for k in range(qt + 1):
                                        lhs = c_ut if k == qt else c_on
                                        ins = t_.matmul(p_pos[qt % 2][:], lhs[:],
                                                        selT[k][:],
                                                        start=(k == 0),
                                                        stop=(k == qt))
                                    p.inc(ins, sTE)
                                    p.M((ph, b, "posmm", qt), sTE)
                            p.sec("tensor", _)

                            def _(v):
                                for qt in range(16):
                                    p.W(v, (ph, b, "posmm", qt))
                                    ins = v.tensor_mul(posP[qt][:],
                                                       p_pos[qt % 2][:], selT[qt][:])
                                    p.inc(ins, sVE)
                                    p.M((ph, b, "posev", qt), sVE)
                                p.M((ph, b, "posall"), sVE)
                            p.sec("vector", _)

                        p.unit(emitS)

                    with ExitStack() as ts_:
                        Et = ts_.enter_context
                        p_qtr = psum_list(Et, "at_qtr", 2, [128, 512], BF16)

                        def emitT():
                            def _(t_):
                                p.W(t_, (ph, b, "ksmm", 7))
                                p.W(t_, (ph, b, "posall"))
                                for m in range(NKT):
                                    for grp in range(4):
                                        gi = m * 4 + grp
                                        if gi >= 2:
                                            p.W(t_, (ph, b, "qtev", gi - 2))
                                        for j in range(4):
                                            tb = grp * 4 + j
                                            ins = t_.transpose(
                                                p_qtr[gi % 2][:, j * 128:(j + 1) * 128],
                                                qd[m][:, tb * 128:(tb + 1) * 128],
                                                c_idbf[:])
                                        p.inc(ins, sTE)
                                        p.M((ph, b, "qtr", gi), sTE)
                            p.sec("tensor", _)

                            def _(v):
                                for m in range(NKT):
                                    for grp in range(4):
                                        gi = m * 4 + grp
                                        p.W(v, (ph, b, "qtr", gi))
                                        for j in range(4):
                                            tb = grp * 4 + j
                                            ins = v.tensor_copy(
                                                tks[tb][:, m * 128:(m + 1) * 128],
                                                p_qtr[gi % 2][:, j * 128:(j + 1) * 128])
                                        p.inc(ins, sV2)
                                        p.M((ph, b, "qtev", gi), sV2)
                                p.M((ph, b, "qtokdone"), sV2)
                            p.sec("vector", _)

                        p.unit(emitT)

                    with ExitStack() as vs_:
                        Ev = vs_.enter_context
                        vmrow = Ev(nc.sbuf_tensor(uname("av_vmrow"), [1, D], F32))
                        vmcbf = Ev(nc.sbuf_tensor(uname("av_vmcbf"), [128, NKT], BF16))
                        wvbf = Ev(nc.sbuf_tensor(uname("av_wvbf"), [1, D], BF16))
                        initb = Ev(nc.sbuf_tensor(uname("av_initb"), [128, D], F32))
                        wvwb = sbuf_list(Ev, "av_wvwb", 2, [128, D], BF16)
                        p_vw = Ev(nc.psum_tensor(uname("av_pvw"), [1, D], F32))
                        p_vmc = Ev(nc.psum_tensor(uname("av_pvmc"), [128, NKT], F32))
                        p_ini = Ev(nc.psum_tensor(uname("av_pini"), [128, D], F32))

                        def emitV():
                            def _(t_):
                                p.W(t_, (ph, b, "tokdone", "V"))
                                p.W(t_, (ph, b, "qtokdone"))
                                for half in range(2):
                                    for st in range(16):
                                        ins = t_.matmul(
                                            p_vw[0:1, half * 512:(half + 1) * 512],
                                            c_inv[:],
                                            vtk[st][:, half * 512:(half + 1) * 512],
                                            start=(st == 0), stop=(st == 15))
                                p.inc(ins, sTE)
                                p.M((ph, b, "vmmm"), sTE)
                                p.W(t_, (ph, b, "vmev"))
                                for k in range(NKT):
                                    ins = t_.transpose(
                                        p_vmc[:, k:k + 1],
                                        vmrow[0:1, k * 128:(k + 1) * 128],
                                        c_idf[0:1, 0:1])
                                p.inc(ins, sTE)
                                p.M((ph, b, "vmtr"), sTE)
                                p.W(t_, (ph, b, "vmcbf"))
                                for half in range(2):
                                    for k in range(NKT):
                                        p.W(t_, (ph, b, "wvw", half, k))
                                        ins = t_.matmul(
                                            p_vw[0:1, half * 512:(half + 1) * 512],
                                            vmcbf[:, k:k + 1],
                                            wvwb[k % 2][:, half * 512:(half + 1) * 512],
                                            start=(k == 0), stop=(k == NKT - 1))
                                        p.inc(ins, sTE)
                                        p.M((ph, b, "wvmm", half, k), sTE)
                                p.W(t_, (ph, b, "wvbf"))
                                for half in range(2):
                                    ins = t_.matmul(
                                        p_ini[:, half * 512:(half + 1) * 512],
                                        c_onr[0:1, :],
                                        wvbf[0:1, half * 512:(half + 1) * 512],
                                        start=True, stop=True)
                                p.inc(ins, sTE)
                                p.M((ph, b, "inimm"), sTE)
                            p.sec("tensor", _)

                            def _(sy):
                                for half in range(2):
                                    for k in range(NKT):
                                        if half == 0 and k == 0:
                                            p.W(sy, (ph, b, "posall"))
                                        gi = half * NKT + k
                                        if gi >= 2:
                                            pk = ((gi - 2) // NKT,
                                                  (gi - 2) % NKT)
                                            p.W(sy, (ph, b, "wvmm",
                                                     pk[0], pk[1]))
                                        ins = sy.dma_start(
                                            out=wvwb[k % 2][:],
                                            in_=h_wob.ap()[k * 128:(k + 1) * 128, :])
                                        p.inc(ins, sWV[gi % 2], 16)
                                        p.M((ph, b, "wvw", half, k), sWV[gi % 2])
                            p.sec("sync", _)

                            def _(v):
                                p.W(v, (ph, b, "vmmm"))
                                ins = v.tensor_copy(vmrow[0:1, :], p_vw[0:1, :])
                                vedge(ins, v)
                                ins = v.tensor_single_scalar(negvm[0:1, :],
                                                             vmrow[0:1, :],
                                                             -1.0, ALU.mult)
                                p.inc(ins, sVE)
                                p.M((ph, b, "vmev"), sVE)
                                p.W(v, (ph, b, "vmtr"))
                                ins = v.tensor_copy(vmcbf[:, 0:NKT],
                                                    p_vmc[:, 0:NKT])
                                p.inc(ins, sVE)
                                p.M((ph, b, "vmcbf"), sVE)
                                p.W(v, (ph, b, "wvmm", 1, NKT - 1))
                                ins = v.tensor_copy(wvbf[0:1, :], p_vw[0:1, :])
                                p.inc(ins, sVE)
                                p.M((ph, b, "wvbf"), sVE)
                                p.W(v, (ph, b, "inimm"))
                                ins = v.tensor_add(initb[:], p_ini[:], c_bo[:])
                                p.inc(ins, sVE)
                                p.M((ph, b, "iniev"), sVE)
                            p.sec("vector", _)

                            def _(g):
                                p.W(g, (ph, b, "iniev"))
                                if b > 0:
                                    p.W(g, (ph, b - 1, "rbdone0"))
                                    p.W(g, (ph, b - 1, "rbdone1"))
                                for t in range(16):
                                    ins = g.dma_start(
                                        out=h_ctxd.ap()[t * 128:(t + 1) * 128, :],
                                        in_=initb[:])
                                    p.inc(ins, sAG, 16)
                                p.M((ph, b, "ctxinit"), sAG)
                            p.sec("gpsimd", _)

                        p.unit(emitV)

                    with ExitStack() as cs_:
                        Ec = cs_.enter_context
                        PTb = Ec(nc.sbuf_tensor(uname("ac_PTb"), [128, 16 * SK], BF16))
                        idxf2 = Ec(nc.sbuf_tensor(uname("ac_idxf"), [SK, 2], F32))
                        qtT = Ec(nc.sbuf_tensor(uname("ac_qtT"), [128, SK], BF16))
                        attns = Ec(nc.sbuf_tensor(uname("ac_attns"), [SK, L], BF16))
                        attnT = Ec(nc.sbuf_tensor(uname("ac_attnT"), [128, 16 * SK], BF16))
                        serow = Ec(nc.sbuf_tensor(uname("ac_serow"), [1, SK], BF16))
                        se45 = Ec(nc.sbuf_tensor(uname("ac_se45"), [SK, 4], F32))
                        otT = Ec(nc.sbuf_tensor(uname("ac_otT"), [128, SK], BF16))
                        ptsb = sbuf_list(Ec, "ac_ptsb", 2, [SK, D], F32)
                        wob2 = sbuf_list(Ec, "ac_wob2", 2, [128, D], BF16)
                        kdb = sbuf_list(Ec, "ac_kdb", 2, [128, L], BF16)
                        cdb = sbuf_list(Ec, "ac_cdb", 2, [128, D], F32)
                        xrb = sbuf_list(Ec, "ac_xrb", 2, [128, D], F32)
                        p_qt = Ec(nc.psum_tensor(uname("ac_pqt"), [128, SK], F32))
                        p_s2 = Ec(nc.psum_tensor(uname("ac_ps2"), [SK, L], F32))
                        p_at = Ec(nc.psum_tensor(uname("ac_pat"), [128, SK], BF16))
                        p_ot = Ec(nc.psum_tensor(uname("ac_pot"), [SK, 128], F32))
                        p_pt = Ec(nc.psum_tensor(uname("ac_ppt"), [SK, 512], F32))

                        def emitC():
                            def _(v):
                                if b == 0:
                                    p.W(v, (ph, b, "ctxinit"))
                                    ins = v.memset(otbuf[0][:, 64:128], 0.0)
                                    ins = v.memset(otbuf[1][:, 0:64], 0.0)
                                    p.inc(ins, sVE)
                                    p.M((ph, "otz"), sVE)
                            p.sec("vector", _)

                            def _(sy):
                                for bh in range(16):
                                    pr, hi = bh // 2, bh % 2
                                    base = 64 * hi
                                    if bh == 0:
                                        p.W(sy, (ph, b, "ctxinit"))
                                        p.W(sy, (ph, b, "kddall0"))
                                        p.W(sy, (ph, b, "kddall1"))
                                    if bh >= 2:
                                        p.W(sy, (ph, b, "st2done", bh - 2))
                                    ins = sy.dma_start(
                                        out=kdb[bh % 2][base:base + 64, :],
                                        in_=h_kdd.ap()[bh * 64:(bh + 1) * 64, :])
                                    p.inc(ins, sKDL[bh % 2], 16)
                                    p.M((ph, b, "kdl", bh), sKDL[bh % 2])
                                    if hi == 0:
                                        if pr >= 2:
                                            p.W(sy, (ph, b, "ptmm",
                                                     (2 * (pr - 2) + 1) * 2 + 1))
                                        ins = sy.dma_start(
                                            out=wob2[pr % 2][:],
                                            in_=h_wob.ap()[pr * 128:(pr + 1) * 128, :])
                                        p.inc(ins, sWOL[pr % 2], 16)
                                        p.M((ph, b, "wol", pr), sWOL[pr % 2])
                            p.sec("sync", _)

                            def _(v):
                                for bh in range(16):
                                    p.W(v, (ph, b, "posall"))
                                    if bh == 0:
                                        p.W(v, (ph, b, "ctxinit"))
                                    if bh >= 1:
                                        p.W(v, (ph, b, "qtopmm", bh - 1))
                                    for qt in range(16):
                                        ins = v.tensor_scalar(
                                            PTb[:, qt * SK:(qt + 1) * SK],
                                            c_io45[:], posP[qt][:, bh:bh + 1],
                                            None, ALU.is_equal)
                                    p.inc(ins, sVE)
                                    p.M((ph, b, "ptb", bh), sVE)
                                    p.W(v, (ph, b, "idxmm", bh))
                                    v.tensor_copy(idxf2[0:SK, :],
                                                  p_ot[0:SK, 112:114])
                                    p.W(v, (ph, b, "st2", bh))
                                    v.reduce_max(se45[:, 0:1], p_s2[:], axis=AX,
                                                 negate=True)
                                    v.scalar_tensor_tensor(
                                        idxf2[0:SK, 0:1], idxf2[0:SK, 1:2],
                                        128.0, idxf2[0:SK, 0:1],
                                        ALU.mult, ALU.add)
                                    ins = v.tensor_single_scalar(
                                        se45[:, 2:3], se45[:, 0:1], 0.125,
                                        ALU.mult)
                                    p.inc(ins, sVE)
                                    p.M((ph, b, "nmx", bh), sVE)
                                    ins = v.tensor_copy(idxi[0:SK, bh:bh + 1],
                                                        idxf2[0:SK, 0:1])
                                    p.inc(ins, sV2)
                                    p.M((ph, b, "idxev", bh), sV2)
                                    p.W(v, (ph, b, "sedone", bh))
                                    ins = v.reciprocal(se45[:, 3:4],
                                                       se45[:, 1:2])
                                    p.inc(ins, sVE)
                                    p.M((ph, b, "rec", bh), sVE)
                                    for st in range(16):
                                        p.W(v, (ph, b, "attr", bh * 16 + st))
                                        ins = v.tensor_copy(
                                            attnT[:, st * SK:(st + 1) * SK],
                                            p_at[:, 0:SK])
                                        p.inc(ins, sVE)
                                        p.M((ph, b, "atev", bh * 16 + st), sVE)
                            p.sec("vector", _)

                            def _(t_):
                                for bh in range(16):
                                    pr, hi = bh // 2, bh % 2
                                    base = 64 * hi
                                    p.W(t_, (ph, b, "ptb", bh))
                                    if bh == 0:
                                        p.W(t_, (ph, b, "iniev"))
                                    if bh >= 1:
                                        p.W(t_, (ph, b, "idxev", bh - 1))
                                        p.W(t_, (ph, b, "otev", bh - 1))
                                    for qt in range(16):
                                        ins = t_.matmul(
                                            p_ot[0:SK, 112:114],
                                            PTb[:, qt * SK:(qt + 1) * SK],
                                            c_io2[:, 2 * qt:2 * qt + 2],
                                            start=(qt == 0), stop=(qt == 15))
                                    p.inc(ins, sTE)
                                    p.M((ph, b, "idxmm", bh), sTE)
                                    p.W(t_, (ph, b, "qtokdone"))
                                    if bh >= 1:
                                        p.W(t_, (ph, b, "qtopev", bh - 1))
                                    for st in range(16):
                                        ins = t_.matmul(
                                            p_qt[:, 0:SK],
                                            tks[st][:, pr * 128:(pr + 1) * 128],
                                            PTb[:, st * SK:(st + 1) * SK],
                                            start=(st == 0), stop=(st == 15))
                                    p.inc(ins, sTE)
                                    p.M((ph, b, "qtopmm", bh), sTE)
                                    p.W(t_, (ph, b, "qtopev", bh))
                                    p.W(t_, (ph, b, "kdl", bh))
                                    if bh >= 1:
                                        p.W(t_, (ph, b, "expdone", bh - 1))
                                    for i in range(4):
                                        ins = t_.matmul(
                                            p_s2[:, i * 512:(i + 1) * 512],
                                            qtT[base:base + 64, 0:SK],
                                            kdb[bh % 2][base:base + 64,
                                                        i * 512:(i + 1) * 512],
                                            start=True, stop=True)
                                    p.inc(ins, sTE)
                                    p.M((ph, b, "st2", bh), sTE)
                                    p.W(t_, (ph, b, "expdone", bh))
                                    if bh >= 1:
                                        p.W(t_, (ph, b, "otTev", bh - 1))
                                    for st in range(16):
                                        if st >= 1:
                                            p.W(t_, (ph, b, "atev", bh * 16 + st - 1))
                                        ins = t_.transpose(
                                            p_at[:, 0:SK],
                                            attns[:, st * 128:(st + 1) * 128],
                                            c_idbf[0:SK, 0:SK])
                                        p.inc(ins, sTE)
                                        p.M((ph, b, "attr", bh * 16 + st), sTE)
                                    p.W(t_, (ph, b, "sedone", bh))
                                    if bh >= 1:
                                        p.W(t_, (ph, b, "otev", bh - 1))
                                    ins = t_.transpose(p_ot[0:1, 64:64 + SK],
                                                       se45[:, 1:2],
                                                       c_idf[0:SK, 0:SK])
                                    p.inc(ins, sTE)
                                    p.M((ph, b, "setr", bh), sTE)
                                    p.W(t_, (ph, b, "serow", bh))
                                    p.W(t_, (ph, b, "idxev", bh))
                                    for st in range(16):
                                        p.W(t_, (ph, b, "atev", bh * 16 + st))
                                        ins = t_.matmul(
                                            p_ot[0:SK, 0:64],
                                            attnT[:, st * SK:(st + 1) * SK],
                                            vtk[st][:, bh * 64:(bh + 1) * 64],
                                            start=(st == 0), stop=False)
                                    ins = t_.matmul(
                                        p_ot[0:SK, 0:64], serow[0:1, 0:SK],
                                        negvm[0:1, bh * 64:(bh + 1) * 64],
                                        start=False, stop=True)
                                    p.inc(ins, sTE)
                                    p.M((ph, b, "otmm", bh), sTE)
                                    p.W(t_, (ph, b, "otev", bh))
                                    ins = t_.transpose(p_at[:, 0:SK],
                                                       otbuf[hi][:],
                                                       c_idbf[0:SK, 0:SK])
                                    p.inc(ins, sTE)
                                    p.M((ph, b, "ottr", bh), sTE)
                                    p.W(t_, (ph, b, "otT", bh))
                                    p.W(t_, (ph, b, "wol", pr))
                                    if bh >= 1:
                                        p.W(t_, (ph, b, "ptev", (bh - 1) * 2 + 1))
                                    for dc in range(2):
                                        if dc == 1:
                                            p.W(t_, (ph, b, "ptev", bh * 2 + 0))
                                        ins = t_.matmul(
                                            p_pt[0:SK, :], otT[:, 0:SK],
                                            wob2[pr % 2][:, dc * 512:(dc + 1) * 512],
                                            start=True, stop=True)
                                        p.inc(ins, sTE)
                                        p.M((ph, b, "ptmm", bh * 2 + dc), sTE)
                            p.sec("tensor", _)

                            def _(s):
                                if b == 0:
                                    p.W(s, (ph, "otz"))
                                for bh in range(16):
                                    pr, hi = bh // 2, bh % 2
                                    p.W(s, (ph, b, "qtopmm", bh))
                                    if bh >= 1:
                                        p.W(s, (ph, b, "st2", bh - 1))
                                    ins = s.activation(qtT[:, 0:SK],
                                                       p_qt[:, 0:SK], AF.Copy)
                                    p.inc(ins, sSE)
                                    p.M((ph, b, "qtopev", bh), sSE)
                                    p.W(s, (ph, b, "nmx", bh))
                                    if bh >= 1:
                                        p.W(s, (ph, b, "attr",
                                                (bh - 1) * 16 + 15))
                                    ins = s.activation(attns[:], p_s2[:], AF.Exp,
                                                       bias=se45[:, 2:3],
                                                       scale=0.125,
                                                       accum_out=se45[:, 1:2])
                                    p.inc(ins, sSE)
                                    p.M((ph, b, "expdone", bh), sSE)
                                    p.M((ph, b, "sedone", bh), sSE)
                                    p.M((ph, b, "st2done", bh), sSE)
                                    p.W(s, (ph, b, "setr", bh))
                                    ins = s.activation(serow[0:1, 0:SK],
                                                       p_ot[0:1, 64:64 + SK],
                                                       AF.Copy)
                                    p.inc(ins, sSE)
                                    p.M((ph, b, "serow", bh), sSE)
                                    p.W(s, (ph, b, "otmm", bh))
                                    p.W(s, (ph, b, "rec", bh))
                                    ins = s.activation(
                                        otbuf[hi][:, 64 * hi:64 * hi + 64],
                                        p_ot[0:SK, 0:64], AF.Identity,
                                        scale=se45[:, 3:4])
                                    p.inc(ins, sSE)
                                    p.M((ph, b, "otev", bh), sSE)
                                    p.W(s, (ph, b, "ottr", bh))
                                    ins = s.activation(otT[:, 0:SK],
                                                       p_at[:, 0:SK], AF.Copy)
                                    p.inc(ins, sSE)
                                    p.M((ph, b, "otTev", bh), sSE)
                                    p.M((ph, b, "otT", bh), sSE)
                                    for dc in range(2):
                                        p.W(s, (ph, b, "ptmm", bh * 2 + dc))
                                        if dc == 0 and bh >= 2:
                                            p.W(s, (ph, b, "scat", bh - 2))
                                        ins = s.activation(
                                            ptsb[bh % 2][0:SK, dc * 512:(dc + 1) * 512],
                                            p_pt[0:SK, :], AF.Copy)
                                        p.inc(ins, sSE)
                                        p.M((ph, b, "ptev", bh * 2 + dc), sSE)
                                    p.M((ph, b, "ptevall", bh), sSE)
                            p.sec("scalar", _)

                            def _(g):
                                for bh in range(16):
                                    p.W(g, (ph, b, "ptevall", bh))
                                    p.W(g, (ph, b, "idxev", bh))
                                    if bh == 0:
                                        p.W(g, (ph, b, "ctxinit"))
                                    else:
                                        p.W(g, (ph, b, "scat", bh - 1))
                                    ins = g.indirect_dma_start(
                                        h_ctxd.ap(),
                                        IndirectOffsetOnAxis(
                                            idxi[0:SK, bh:bh + 1], 0),
                                        ptsb[bh % 2][0:SK, :], None,
                                        compute_op=ALU.add)
                                    p.inc(ins, sSC, 16)
                                    p.M((ph, b, "scat", bh), sSC)
                                p.W(g, (ph, b, "scat", 15))
                                for t in range(18):
                                    if t >= 2:
                                        p.W(g, (ph, b, "rbadd", t - 2))
                                        ins = g.dma_start(
                                            out=dst_ap[row0 + (t - 2) * 128:
                                                       row0 + (t - 1) * 128, :],
                                            in_=xrb[t % 2][:])
                                        p.inc(ins, sRS[t % 2], 16)
                                        p.M((ph, b, "rbs", t - 2), sRS[t % 2])
                                    if t < 16:
                                        if t >= 2:
                                            p.W(g, (ph, b, "rbs", t - 2))
                                        ins = g.dma_start(
                                            out=cdb[t % 2][:],
                                            in_=h_ctxd.ap()[t * 128:(t + 1) * 128, :])
                                        p.inc(ins, sRB[t % 2], 16)
                                        ins = g.dma_start(
                                            out=xrb[t % 2][:],
                                            in_=src_ap[row0 + t * 128:
                                                       row0 + (t + 1) * 128, :])
                                        p.inc(ins, sRB[t % 2], 16)
                                        p.M((ph, b, "rbl", t), sRB[t % 2])
                                p.M((ph, b, "rbdone0"), sRS[0])
                                p.M((ph, b, "rbdone1"), sRS[1])
                            p.sec("gpsimd", _)

                            def _(v):
                                for t in range(16):
                                    p.W(v, (ph, b, "rbl", t))
                                    ins = v.tensor_add(xrb[t % 2][:], xrb[t % 2][:],
                                                       cdb[t % 2][:])
                                    p.inc(ins, sVE)
                                    p.M((ph, b, "rbadd", t), sVE)
                            p.sec("vector", _)

                            if _DBGHW and b == _DBGB:
                                def _(g):
                                    p.W(g, (ph, b, "mrowdone"))
                                    ins = g.dma_start(out=h_dm.ap(), in_=mrow[:])
                                    p.inc(ins, sAG, 16)
                                    p.W(g, (ph, b, "sel"))
                                    ins = g.dma_start(out=h_ds.ap(), in_=srch[:])
                                    p.inc(ins, sAG, 16)
                                    p.W(g, (ph, b, "idxev", 15))
                                    ins = g.dma_start(out=h_di.ap(),
                                                      in_=idxi[0:SK, :])
                                    p.inc(ins, sAG, 16)
                                    p.W(g, (ph, b, "vmev"))
                                    ins = g.dma_start(out=h_dn.ap(),
                                                      in_=negvm[0:1, :])
                                    p.inc(ins, sAG, 16)
                                    p.W(g, (ph, b, "iniev"))
                                    ins = g.dma_start(out=h_dini.ap(),
                                                      in_=initb[:])
                                    p.inc(ins, sAG, 16)
                                    p.W(g, (ph, b, "expdone", 15))
                                    ins = g.dma_start(out=h_da.ap(), in_=attns[:])
                                    p.inc(ins, sAG, 16)
                                    p.W(g, (ph, b, "qtopev", 15))
                                    ins = g.dma_start(out=h_dq.ap(),
                                                      in_=qtT[:, 0:SK])
                                    p.inc(ins, sAG, 16)
                                    p.W(g, (ph, b, "ptev", 31))
                                    ins = g.dma_start(out=h_dpt.ap(),
                                                      in_=ptsb[1][0:SK, :])
                                    p.inc(ins, sAG, 16)
                                    p.W(g, (ph, b, "kdl", 15))
                                    ins = g.dma_start(out=h_dkd.ap(),
                                                      in_=kdb[1][:])
                                    p.inc(ins, sAG, 16)
                                    p.W(g, (ph, b, "scat", 15))
                                    ins = g.dma_start(out=h_dc.ap(),
                                                      in_=h_ctxd.ap())
                                    p.inc(ins, sAG, 16)
                                    p.W(g, (ph, b, "rec", 15))
                                    ins = g.dma_start(out=h_dse.ap(),
                                                      in_=se45[:])
                                    p.inc(ins, sAG, 16)
                                    p.W(g, (ph, b, "otev", 15))
                                    ins = g.dma_start(out=h_dob.ap(),
                                                      in_=otbuf[1][:])
                                    p.inc(ins, sAG, 16)
                                    p.W(g, (ph, b, "otT", 15))
                                    ins = g.dma_start(out=h_dot.ap(),
                                                      in_=otT[:])
                                    p.inc(ins, sAG, 16)
                                    p.W(g, (ph, b, "wol", 7))
                                    ins = g.dma_start(out=h_dwo.ap(),
                                                      in_=wob2[1][:])
                                    p.inc(ins, sAG, 16)
                                p.sec("gpsimd", _)

                        p.unit(emitC)
                    batch_guards[b] = [(s, s.v) for s in sems]
            return [(s, s.v) for s in sems]

        if sublayers == 1:
            run_ffn("F1", h_x.ap(), h_y.ap(), prologue_guard)
        elif sublayers == 2:
            g1 = run_ffn("F1", h_x.ap(), h_xc1.ap(), prologue_guard)
            run_attn("A", h_xc1.ap(), h_y.ap(), g1)
        else:
            g1 = run_ffn("F1", h_x.ap(), h_xc1.ap(), prologue_guard)
            g2 = run_attn("A", h_xc1.ap(), h_xc2.ap(), g1)
            run_ffn("F2", h_xc2.ap(), h_y.ap(), g2)

        def _(g):
            for s in sems:
                if s.v > 0:
                    g.wait_ge(s.h, s.v)
        block.gpsimd(_)

    return nc


_PROG_CACHE = {}


def _get_prog():
    key = _SUBLAYERS
    if key not in _PROG_CACHE:
        _PROG_CACHE[key] = build_program(key)
    return _PROG_CACHE[key]


def _constants():
    c = {}
    c["ident_bf"] = np.eye(128, dtype=BF)
    c["ident_f32"] = np.eye(128, dtype=np.float32)
    c["iota45"] = np.broadcast_to(
        np.arange(1, SK + 1, dtype=np.float32), (128, SK)).copy()
    io2 = np.zeros((128, 32), np.float32)
    for qt in range(16):
        io2[:, 2 * qt] = np.arange(128)
        io2[:, 2 * qt + 1] = qt
    c["iota2"] = io2.astype(BF)
    c["iotaq16"] = (np.arange(128, dtype=np.float32)[:, None]
                    + 128.0 * np.arange(16, dtype=np.float32)[None, :]).copy()
    c["ut128"] = np.triu(np.ones((128, 128), np.float32)).astype(BF)
    c["ones128"] = np.ones((128, 128), BF)
    c["onesrow"] = np.ones((1, 128), BF)
    c["invcol"] = np.full((128, 1), 1.0 / L, np.float32).astype(BF)
    c["epscol"] = np.full((128, 1), EPS, np.float32)
    return c


def kernel(L0_attn_x, L0_attn_Wq, L0_attn_bq, L0_attn_Wk, L0_attn_bk,
           L0_attn_Wv, L0_attn_bv, L0_attn_Wo, L0_attn_bo, L0_attn_sample_idx,
           L0_ffn_gate_w, L0_ffn_w1, L0_ffn_w2):
    nc = _get_prog()
    x = np.ascontiguousarray(np.asarray(L0_attn_x, np.float32))
    shared = {
        "w1": np.ascontiguousarray(np.asarray(L0_ffn_w1, np.float32)),
        "w2": np.ascontiguousarray(np.asarray(L0_ffn_w2, np.float32)),
        "gwT": np.ascontiguousarray(np.asarray(L0_ffn_gate_w, np.float32).T),
        "wqT": np.ascontiguousarray(np.asarray(L0_attn_Wq, np.float32).T),
        "wkT": np.ascontiguousarray(np.asarray(L0_attn_Wk, np.float32).T),
        "wvT": np.ascontiguousarray(np.asarray(L0_attn_Wv, np.float32).T),
        "woT": np.ascontiguousarray(np.asarray(L0_attn_Wo, np.float32).T),
        "bq": np.ascontiguousarray(np.asarray(L0_attn_bq, np.float32)
                                   .reshape(NKT, 128).T),
        "bk": np.ascontiguousarray(np.asarray(L0_attn_bk, np.float32)
                                   .reshape(NKT, 128).T),
        "bv": np.ascontiguousarray(np.asarray(L0_attn_bv, np.float32)
                                   .reshape(NKT, 128).T),
        "bo_b": np.ascontiguousarray(np.broadcast_to(
            np.asarray(L0_attn_bo, np.float32)[None, :], (128, D))),
        "sidx_b": np.ascontiguousarray(np.broadcast_to(
            np.asarray(L0_attn_sample_idx, np.int32)[None, :], (128, SK))),
    }
    shared.update(_constants())
    in_maps = []
    for c in range(NCORES):
        m = dict(shared)
        m["x"] = np.ascontiguousarray(
            x[c * BPC:(c + 1) * BPC].reshape(TPC, D))
        in_maps.append(m)
    res = run_bass_kernel_spmd(nc, in_maps, core_ids=list(range(NCORES)))
    out = np.concatenate(
        [np.asarray(r["y"], np.float32).reshape(BPC, L, D)
         for r in res.results], axis=0)
    return out
